# version 18
# baseline (speedup 1.0000x reference)
import numpy as np
import ml_dtypes
from contextlib import ExitStack

import concourse.bass as bass
import concourse.mybir as mybir
from concourse.bass_utils import run_bass_kernel_spmd

F32 = mybir.dt.float32
BF16 = mybir.dt.bfloat16
I32 = mybir.dt.int32
AF = mybir.ActivationFunctionType
ALU = mybir.AluOpType
AX = mybir.AxisListType
bf16 = ml_dtypes.bfloat16

C = 4096
S = 2048
TOWN = 1024
NA = 32
NJ = 24
ND = 8
NF = 86
NM = 32
LT = 128
NLT = TOWN // LT
FGROUPS = [(0, 29), (29, 29), (58, 28)]
EPS = 1e-5
SCALE = 1.0 / 32.0
KN = 128 * ND * TOWN
BLOB = 2 * KN


class Ctr:
    def __init__(self, nc, es, name):
        self.h = es.enter_context(nc.semaphore(name))
        self.n = 0

    def inc(self, instr, k=1):
        instr.then_inc(self.h, k)
        self.n += k
        return self.n

    def wait(self, eng, val):
        if val > 0:
            eng.wait_ge(self.h, val)


class DmaStream:
    def __init__(self, nc, es, name, depth=2):
        self.sems = [es.enter_context(nc.semaphore(f"{name}{i}"))
                     for i in range(depth)]
        self.cnt = [0] * depth
        self.depth = depth

    def issue(self, instr, k=0):
        i = k % self.depth
        instr.then_inc(self.sems[i], 16)
        self.cnt[i] += 16
        return (i, self.cnt[i])

    def wait(self, eng, mark):
        if mark:
            eng.wait_ge(self.sems[mark[0]], mark[1])

    def wait_all(self, eng):
        for i in range(self.depth):
            if self.cnt[i]:
                eng.wait_ge(self.sems[i], self.cnt[i])


def build(debug=False):
    nc = bass.Bass("TRN2", target_bir_lowering=False)

    x_fm = nc.dram_tensor("x_fm", [C, TOWN], F32, kind="ExternalInput").ap()
    x_st = nc.dram_tensor("x_st", [TOWN // LT, 128, NA, LT], F32,
                          kind="ExternalInput").ap()
    wqkv_st = nc.dram_tensor("wqkv_st", [NJ, 128, NA, 128], F32, kind="ExternalInput").ap()
    bqkv_st = nc.dram_tensor("bqkv_st", [128, NJ], F32, kind="ExternalInput").ap()
    wo_st = nc.dram_tensor("wo_st", [NM, 128, ND, 128], F32, kind="ExternalInput").ap()
    bo_st = nc.dram_tensor("bo_st", [128, NM], F32, kind="ExternalInput").ap()
    wg_st = nc.dram_tensor("wg_st", [NF, 128, NA, 128], I32, kind="ExternalInput").ap()
    wu_st = nc.dram_tensor("wu_st", [NF, 128, NA, 128], I32, kind="ExternalInput").ap()
    wd_st = nc.dram_tensor("wd_st", [NM, 128, NF, 128], I32, kind="ExternalInput").ap()
    gs_st = nc.dram_tensor("gs_st", [128, NF], F32, kind="ExternalInput").ap()
    us_st = nc.dram_tensor("us_st", [128, NF], F32, kind="ExternalInput").ap()
    ds_st = nc.dram_tensor("ds_st", [128, NM], F32, kind="ExternalInput").ap()
    ident = nc.dram_tensor("ident", [128, 128], BF16, kind="ExternalInput").ap()

    out_fm = nc.dram_tensor("out_fm", [C, TOWN], F32, kind="ExternalOutput").ap()

    scr = "ExternalOutput" if debug else "Internal"
    xn_sp = nc.dram_tensor("xn_sp", [C, TOWN], BF16, kind=scr).ap()
    s_sp = nc.dram_tensor("s_sp", [C, TOWN], BF16, kind=scr).ap()
    acc_sp = nc.dram_tensor("acc_sp", [C, TOWN], F32, kind=scr).ap()
    kv_loc = nc.dram_tensor("kv_loc", [1, BLOB], BF16, kind="Internal").ap()
    kv_all = nc.dram_tensor("kv_all", [8, BLOB], BF16, kind="Internal",
                            addr_space="Shared").ap()
    kvl_r = kv_loc.rearrange("r (s p j t) -> (r s) p j t", s=2, p=128, j=ND)
    xn_q0 = xn_sp.rearrange("(a p) t -> p a t", p=128)

    with ExitStack() as es0:
        sb0 = lambda n, s_, d: es0.enter_context(nc.sbuf_tensor(n, s_, d))
        bqkv_sb = sb0("bqkv_sb", [128, NJ], F32)
        bo_sb = sb0("bo_sb", [128, NM], F32)
        gs_sb = sb0("gs_sb", [128, NF], F32)
        us_sb = sb0("us_sb", [128, NF], F32)
        ds_sb = sb0("ds_sb", [128, NM], F32)
        id_sb = sb0("id_sb", [128, 128], BF16)
        onesm_sb = sb0("onesm_sb", [128, 128], BF16)
        eps_sb = sb0("eps_sb", [128, 1], F32)

        p = Ctr(nc, es0, "p")
        a = Ctr(nc, es0, "a")
        v = Ctr(nc, es0, "v")
        co = Ctr(nc, es0, "co")
        ys = DmaStream(nc, es0, "ys", 1)
        yx = DmaStream(nc, es0, "yx", 4)
        yn = DmaStream(nc, es0, "yn", 2)
        yw = DmaStream(nc, es0, "yw", 2)
        ye = DmaStream(nc, es0, "ye", 2)
        yk = DmaStream(nc, es0, "yk", 1)
        zs = DmaStream(nc, es0, "zs", 1)
        yA = DmaStream(nc, es0, "yA", 2)
        zA = DmaStream(nc, es0, "zA", 2)
        gw = DmaStream(nc, es0, "gw", 2)
        gwo = DmaStream(nc, es0, "gwo", 4)
        gd = DmaStream(nc, es0, "gd", 2)
        ga = DmaStream(nc, es0, "ga", 2)

        q_es = ExitStack()
        q_sb = q_es.enter_context(
            nc.sbuf_tensor("q_sb", [128, ND, TOWN], BF16, side="right"))
        xn1_es = ExitStack()
        xn_t1 = xn1_es.enter_context(
            nc.sbuf_tensor("xn_t1", [128, NA, TOWN], BF16, side="right"))
        wq_es = ExitStack()
        wq_buf = [wq_es.enter_context(
            nc.sbuf_tensor(f"wqb{i}", [128, NA, 128], BF16, side="right"))
            for i in range(3)]
        m_wq_pre = [None, None]

        lnb_es = ExitStack()
        xsq = [lnb_es.enter_context(
            nc.sbuf_tensor(f"xsq{i}", [128, NA, LT], BF16))
            for i in range(NLT)]
        m_xld = [None] * NLT

        with nc.Block(no_gpsimd_drain=True) as blk:
            @blk.sync
            def _(sync):
                for dst, src in ((bqkv_sb, bqkv_st), (bo_sb, bo_st),
                                 (gs_sb, gs_st), (us_sb, us_st),
                                 (ds_sb, ds_st), (id_sb, ident)):
                    ys.issue(sync.dma_start(dst[:], src))
                ys.wait_all(sync)

            @blk.vector
            def _(vec):
                vec.memset(onesm_sb[:], 1.0)
                vec.memset(eps_sb[:], EPS)

            @blk.gpsimd
            def _(gp):
                for t0 in range(3):
                    m_xld[t0] = yx.issue(
                        gp.dma_start(xsq[t0][:], x_st[t0]), t0)
                m_wq_pre[0] = gw.issue(
                    gp.dma_start(wq_buf[0][:], wqkv_st[0]), 0)
                for t0 in range(3, 5):
                    m_xld[t0] = yx.issue(
                        gp.dma_start(xsq[t0][:], x_st[t0]), t0)
                m_wq_pre[1] = gw.issue(
                    gp.dma_start(wq_buf[1][:], wqkv_st[1]), 1)
                for t0 in range(5, NLT):
                    m_xld[t0] = yx.issue(
                        gp.dma_start(xsq[t0][:], x_st[t0]), t0)

        with ExitStack() as es1:
            sb = lambda n, s_, d: es1.enter_context(nc.sbuf_tensor(n, s_, d))
            ps = lambda n, s_, d: es1.enter_context(nc.psum_tensor(n, s_, d))
            sqb = [sb(f"sqb{i}", [128, NA, LT], BF16) for i in range(2)]
            tmp_f = sb("tmp_f", [128, NA, LT], BF16)
            mm_r = [sb(f"mm{i}", [128, 2 * LT], F32) for i in range(4)]
            nm2_r = [sb(f"nm2_{i}", [128, LT], F32) for i in range(4)]
            ab_r = [sb(f"abr{i}", [128, 2 * LT], F32) for i in range(4)]
            stat_ps = [ps(f"statps{i}", [128, 2 * LT], F32) for i in range(4)]

            xn_r = xn_sp.rearrange("(a p) (n t) -> n p a t", p=128, t=LT)

            m_xnw = [None] * NLT
            m_sq = [0] * NLT
            m_st = [0] * NLT; m_vm = [0] * NLT; m_sqrt = [0] * NLT
            m_ab = [0] * NLT; m_bc = [0] * NLT; m_abe = [0] * NLT
            m_apm = [0] * NLT; m_ap = [0] * NLT

            with nc.Block(no_gpsimd_drain=True) as blk:
                @blk.scalar
                def _(sc):
                    sc.preload_act_table(AF.Sqrt)

                def emit_front(t):
                    tb = t % 4

                    @blk.scalar
                    def _(sc, t=t):
                        yx.wait(sc, m_xld[t])
                        if t >= 2:
                            p.wait(sc, m_st[t - 2])
                        m_sq[t] = a.inc(sc.activation(
                            sqb[t % 2][:], xsq[t][:], AF.Square))

                    @blk.tensor
                    def _(te, t=t, tb=tb):
                        yx.wait(te, m_xld[t])
                        if t >= 4:
                            v.wait(te, m_vm[t - 4])
                        for ca in range(NA):
                            mm = te.matmul(stat_ps[tb][:, 0:LT], onesm_sb[:],
                                           xsq[t][:, ca, :],
                                           start=(ca == 0), stop=(ca == NA - 1))
                        a.wait(te, m_sq[t])
                        for ca in range(NA):
                            mm = te.matmul(stat_ps[tb][:, LT:2 * LT],
                                           onesm_sb[:], sqb[t % 2][:, ca, :],
                                           start=(ca == 0), stop=(ca == NA - 1))
                        m_st[t] = p.inc(mm)

                    @blk.vector
                    def _(vec, t=t, tb=tb):
                        p.wait(vec, m_st[t])
                        if t >= 4:
                            a.wait(vec, m_sqrt[t - 4])
                            v.wait(vec, m_ab[t - 4])
                        i1 = v.inc(vec.tensor_scalar(
                            mm_r[tb][:], stat_ps[tb][:],
                            1.0 / C, None, ALU.mult))
                        v.wait(vec, i1)
                        i2 = v.inc(vec.scalar_tensor_tensor(
                            nm2_r[tb][:], mm_r[tb][:, 0:LT], -1.0,
                            mm_r[tb][:, 0:LT], ALU.mult, ALU.mult))
                        v.wait(vec, i2)
                        m_vm[t] = v.inc(vec.tensor_add(
                            nm2_r[tb][:], mm_r[tb][:, LT:2 * LT],
                            nm2_r[tb][:]))

                def emit_back(t):
                    tb = t % 4

                    @blk.scalar
                    def _(sc, t=t, tb=tb):
                        v.wait(sc, m_vm[t])
                        m_sqrt[t] = a.inc(sc.activation(
                            nm2_r[tb][:], nm2_r[tb][:], AF.Sqrt,
                            bias=eps_sb[:, 0:1]))

                    @blk.vector
                    def _(vec, t=t, tb=tb):
                        a.wait(vec, m_sqrt[t])
                        if t >= 4:
                            v.wait(vec, m_ap[t - 4])
                        i3 = v.inc(vec.reciprocal(ab_r[tb][:, 0:LT],
                                                  nm2_r[tb][:]))
                        v.wait(vec, i3)
                        m_ab[t] = v.inc(vec.scalar_tensor_tensor(
                            ab_r[tb][:, LT:2 * LT], mm_r[tb][:, 0:LT], -1.0,
                            ab_r[tb][:, 0:LT], ALU.mult, ALU.mult))

                    @blk.vector
                    def _(vec, t=t, tb=tb):
                        v.wait(vec, m_ab[t])
                        m_apm[t] = v.inc(vec.tensor_tensor(
                            tmp_f[:], xsq[t][:],
                            ab_r[tb][:, 0:LT].unsqueeze(1)
                            .broadcast_to([128, NA, LT]), ALU.mult))
                        v.wait(vec, m_apm[t])
                        m_ap[t] = v.inc(vec.tensor_tensor(
                            xn_t1[:, :, t * LT:(t + 1) * LT], tmp_f[:],
                            ab_r[tb][:, LT:2 * LT].unsqueeze(1)
                            .broadcast_to([128, NA, LT]), ALU.add))

                    @blk.sync
                    def _(sync, t=t):
                        v.wait(sync, m_ap[t])
                        m_xnw[t] = yn.issue(sync.dma_start(
                            xn_r[t], xn_t1[:, :, t * LT:(t + 1) * LT]), t)

                for t in range(NLT):
                    emit_front(t)
                    if t >= 1:
                        emit_back(t - 1)
                emit_back(NLT - 1)
        lnb_es.close()

        esH = ExitStack()
        sbh = lambda n, s_, d: esH.enter_context(nc.sbuf_tensor(n, s_, d))
        wgb = [sbh(f"wgb{i}", [128, NA, 128], BF16) for i in range(2)]
        wub = [sbh(f"wub{i}", [128, NA, 128], BF16) for i in range(2)]
        wdb = [sbh(f"wdb{i}", [128, 29, 128], BF16) for i in range(2)]
        m_wgu_all = [[None] * na for (a0, na) in FGROUPS]
        m_wd_all = [[None] * NM for _ in FGROUPS]
        last_gu_mark = [0]
        last_dn_mark = [0]
        m_xno2 = [None]


        with ExitStack() as esQ:
            sbq = lambda n, s_, d: esQ.enter_context(nc.sbuf_tensor(n, s_, d))
            psq = lambda n, s_, d: esQ.enter_context(nc.psum_tensor(n, s_, d))
            k_own = sbq("k_own", [128, ND, TOWN], BF16)
            v_stage = sbq("v_stage", [128, ND, TOWN], BF16)
            v_tmo = sbq("v_tmo", [128, ND, TOWN], BF16)
            ps_qkv = [psq(f"psqkv{i}", [128, TOWN], F32) for i in range(2)]
            tr_ps = [psq(f"trps{i}", [128, 128], BF16) for i in range(2)]

            m_wq = [None] * NJ
            m_wq[0], m_wq[1] = m_wq_pre[0], m_wq_pre[1]
            m_qmm = [0] * NJ; m_qev = [0] * NJ
            m_vev = [0] * ND
            m_vtr = []; m_vcp = []

            with nc.Block(no_gpsimd_drain=True) as blk:
                for i in range(NJ):
                    j = i

                    if i >= 2:
                        @blk.gpsimd
                        def _(gp, i=i, j=j):
                            if i >= 3:
                                p.wait(gp, m_qmm[i - 3])
                            m_wq[i] = gw.issue(gp.dma_start(wq_buf[i % 3][:],
                                                            wqkv_st[j]), i)

                    @blk.tensor
                    def _(te, i=i):
                        gw.wait(te, m_wq[i])
                        if i >= 2:
                            v.wait(te, m_qev[i - 2])
                        for ks in (0, 512):
                            for ca in range(NA):
                                mm = te.matmul(
                                    ps_qkv[i % 2][:, ks:ks + 512],
                                    wq_buf[i % 3][:, ca, :],
                                    xn_t1[:, ca, ks:ks + 512],
                                    start=(ca == 0), stop=(ca == NA - 1))
                        m_qmm[i] = p.inc(mm)

                    @blk.vector
                    def _(vec, i=i, j=j):
                        p.wait(vec, m_qmm[i])
                        if j < 8:
                            m_qev[i] = v.inc(vec.tensor_scalar(
                                q_sb[:, j, :], ps_qkv[i % 2][:],
                                bqkv_sb[:, j:j + 1], SCALE,
                                ALU.add, ALU.mult))
                        else:
                            dst = (k_own[:, j - 8, :] if j < 16
                                   else v_stage[:, j - 16, :])
                            m_qev[i] = v.inc(vec.tensor_scalar(
                                dst, ps_qkv[i % 2][:],
                                bqkv_sb[:, j:j + 1], None, ALU.add))
                        if j >= 16:
                            m_vev[j - 16] = m_qev[i]

                    if i == NJ - 1:
                        for jd in range(ND):
                            for kb in range(ND):
                                ti = jd * ND + kb

                                @blk.tensor
                                def _(te, jd=jd, kb=kb, ti=ti):
                                    v.wait(te, m_vev[jd])
                                    if ti >= 2:
                                        v.wait(te, m_vcp[ti - 2])
                                    m_vtr.append(p.inc(te.transpose(
                                        tr_ps[ti % 2][:],
                                        v_stage[:, jd, kb * 128:(kb + 1) * 128],
                                        id_sb[:])))

                                @blk.vector
                                def _(vec, jd=jd, kb=kb, ti=ti):
                                    p.wait(vec, m_vtr[ti])
                                    m_vcp.append(v.inc(vec.tensor_copy(
                                        v_tmo[:, kb, jd * 128:(jd + 1) * 128],
                                        tr_ps[ti % 2][:])))

                @blk.sync
                def _(sync):
                    for i8 in range(8, 16):
                        v.wait(sync, m_qev[i8])
                    yk.issue(sync.dma_start(kvl_r[0], k_own[:]))
                    v.wait(sync, v.n)
                    yk.issue(sync.dma_start(kvl_r[1], v_tmo[:]))
                    yk.wait_all(sync)
                    yn.wait_all(sync)

                @blk.gpsimd
                def _(gp):
                    p.wait(gp, m_qmm[NJ - 6])
                    j0 = FGROUPS[0][0]
                    gw.issue(gp.dma_start(wgb[0][:], wg_st[j0]), 0)
                    m_wgu_all[0][0] = gw.issue(gp.dma_start(wub[0][:],
                                                            wu_st[j0]), 0)
                    gw.issue(gp.dma_start(wgb[1][:], wg_st[j0 + 1]), 1)
                    m_wgu_all[0][1] = gw.issue(gp.dma_start(wub[1][:],
                                                            wu_st[j0 + 1]), 1)
                    yk.wait_all(gp)
                    co.inc(gp.collective_compute(
                        "AllGather", ALU.bypass,
                        replica_groups=[list(range(8))],
                        ins=[kv_loc.opt()], outs=[kv_all.opt()]))

        wq_es.close()

        def emit_gu(fgi, xn_buf, m_xn_ready):
            a0, na = FGROUPS[fgi]
            m_wgu = m_wgu_all[fgi]
            m_wd = m_wd_all[fgi]

            with ExitStack() as esG:
                sbg = lambda n, s_, d: esG.enter_context(nc.sbuf_tensor(n, s_, d))
                psg = lambda n, s_, d: esG.enter_context(nc.psum_tensor(n, s_, d))
                gtmp = [sbg(f"gtmp{fgi}_{i}", [128, TOWN], F32) for i in range(2)]
                pg = [psg(f"pg{fgi}_{i}", [128, TOWN], F32) for i in range(2)]
                pu = [psg(f"pu{fgi}_{i}", [128, TOWN], F32) for i in range(2)]

                m_gumm = [0] * na; m_sil = [0] * na; m_h = [0] * na

                with nc.Block(no_gpsimd_drain=True) as blk:
                    @blk.scalar
                    def _(sc):
                        sc.preload_act_table(AF.Silu)

                    for jj in range(na):
                        j = a0 + jj

                        if jj >= 2:
                            @blk.gpsimd
                            def _(gp, jj=jj, j=j):
                                p.wait(gp, m_gumm[jj - 2])
                                gw.issue(gp.dma_start(wgb[jj % 2][:],
                                                      wg_st[j]), jj)
                                m_wgu[jj] = gw.issue(gp.dma_start(
                                    wub[jj % 2][:], wu_st[j]), jj)

                        if jj == na - 10:
                            @blk.gpsimd
                            def _(gp, fgi=fgi, na=na, a0=a0):
                                if fgi > 0:
                                    p.wait(gp, last_dn_mark[0])
                                for mq in (0, 1):
                                    m_wd[mq] = gd.issue(gp.dma_start(
                                        wdb[mq][:, 0:na, :],
                                        wd_st[mq][:, a0:a0 + na, :]), mq)

                        @blk.tensor
                        def _(te, jj=jj):
                            gw.wait(te, m_wgu[jj])
                            if jj >= 2:
                                a.wait(te, m_sil[jj - 2])
                                v.wait(te, m_h[jj - 2])
                            for buf, wb in ((pg, wgb), (pu, wub)):
                                for ks in (0, 512):
                                    for ca in range(NA):
                                        if jj == 0 and m_xn_ready \
                                                and ca % 8 == 0 and ks == 0 \
                                                and buf is pg:
                                            (yA if (ca // 8) % 2 == 0 else zA) \
                                                .wait(te, m_xn_ready[ca // 8])
                                        mm = te.matmul(
                                            buf[jj % 2][:, ks:ks + 512],
                                            wb[jj % 2][:, ca, :],
                                            xn_buf[:, ca, ks:ks + 512],
                                            start=(ca == 0), stop=(ca == NA - 1))
                            m_gumm[jj] = p.inc(mm)

                        @blk.scalar
                        def _(sc, jj=jj, j=j):
                            p.wait(sc, m_gumm[jj])
                            if jj >= 2:
                                v.wait(sc, m_h[jj - 2])
                            m_sil[jj] = a.inc(sc.activation(
                                gtmp[jj % 2][:], pg[jj % 2][:], AF.Silu,
                                scale=gs_sb[:, j:j + 1]))

                        @blk.vector
                        def _(vec, jj=jj, j=j):
                            a.wait(vec, m_sil[jj])
                            m_h[jj] = v.inc(vec.scalar_tensor_tensor(
                                h_sb[:, jj, :], pu[jj % 2][:],
                                us_sb[:, j:j + 1], gtmp[jj % 2][:],
                                ALU.mult, ALU.mult))

                last_gu_mark[0] = m_gumm[na - 1]

        def emit_dn(fgi):
            a0, na = FGROUPS[fgi]
            m_wd = m_wd_all[fgi]
            last = fgi == len(FGROUPS) - 1

            with ExitStack() as esD:
                sbd = lambda n, s_, d: esD.enter_context(nc.sbuf_tensor(n, s_, d))
                psd = lambda n, s_, d: esD.enter_context(nc.psum_tensor(n, s_, d))
                pd = [psd(f"pd{fgi}_{i}", [128, TOWN], F32) for i in range(2)]
                m_dmm = [0] * NM

                if not last:
                    dtmp = [sbd(f"dtmp{fgi}_{i}", [128, TOWN], F32)
                            for i in range(2)]
                    m_dev = [0] * NM; m_accw = [None] * NM
                else:
                    acc_in = [sbd(f"acci{i}", [128, TOWN], F32) for i in range(2)]
                    s_in = [sbd(f"sin{i}", [128, TOWN], BF16) for i in range(2)]
                    x_in = [sbd(f"xin{i}", [128, TOWN], F32) for i in range(2)]
                    out_t = [sbd(f"outt{i}", [128, TOWN], F32) for i in range(2)]
                    t2 = [sbd(f"t2{i}", [128, TOWN], F32) for i in range(2)]
                    gt = [sbd(f"gt{i}", [128, TOWN], F32) for i in range(2)]
                    sg = sbd("sg", [128, TOWN], F32)
                    m_t2 = [0] * NM; m_gt = [0] * NM
                    m_vout = [0] * NM; m_outw = [None] * NM
                    m_ep = [None] * NM

                with nc.Block(no_gpsimd_drain=True) as blk:
                    def load_wd(gp, m):
                        m_wd[m] = gd.issue(gp.dma_start(
                            wdb[m % 2][:, 0:na, :],
                            wd_st[m][:, a0:a0 + na, :]), m)

                    if last:
                        @blk.sync
                        def _(sync):
                            ga.wait_all(sync)
                            for m in (0, 1):
                                ye.issue(sync.dma_start(
                                    acc_in[m][:],
                                    acc_sp[m * 128:(m + 1) * 128, :]), m)
                                ye.issue(sync.dma_start(
                                    s_in[m][:],
                                    s_sp[m * 128:(m + 1) * 128, :]), m)
                                m_ep[m] = ye.issue(sync.dma_start(
                                    x_in[m][:],
                                    x_fm[m * 128:(m + 1) * 128, :]), m)

                    for m in range(NM):
                        @blk.tensor
                        def _(te, m=m):
                            gd.wait(te, m_wd[m])
                            if m >= 2:
                                if last:
                                    v.wait(te, m_t2[m - 2])
                                else:
                                    v.wait(te, m_dev[m - 2])
                            for ks in (0, 512):
                                for aa in range(na):
                                    mm = te.matmul(
                                        pd[m % 2][:, ks:ks + 512],
                                        wdb[m % 2][:, aa, :],
                                        h_sb[:, aa, ks:ks + 512],
                                        start=(aa == 0), stop=(aa == na - 1))
                            m_dmm[m] = p.inc(mm)

                        if not last and m == 20:
                            @blk.gpsimd
                            def _(gp, fgi=fgi, a0_next=FGROUPS[fgi + 1][0]):
                                p.wait(gp, last_gu_mark[0])
                                for jq in (0, 1):
                                    gw.issue(gp.dma_start(
                                        wgb[jq][:], wg_st[a0_next + jq]), jq)
                                    m_wgu_all[fgi + 1][jq] = gw.issue(
                                        gp.dma_start(wub[jq][:],
                                                     wu_st[a0_next + jq]), jq)

                        if not last:
                            @blk.vector
                            def _(vec, m=m):
                                p.wait(vec, m_dmm[m])
                                if m >= 2:
                                    if fgi == 0:
                                        yw.wait(vec, m_accw[m - 2])
                                    else:
                                        ga.wait(vec, m_accw[m - 2])
                                m_dev[m] = v.inc(vec.tensor_copy(
                                    dtmp[m % 2][:], pd[m % 2][:]))

                            if fgi == 0:
                                @blk.sync
                                def _(sync, m=m):
                                    v.wait(sync, m_dev[m])
                                    m_accw[m] = yw.issue(sync.dma_start(
                                        acc_sp[m * 128:(m + 1) * 128, :],
                                        dtmp[m % 2][:]), m)

                                @blk.gpsimd
                                def _(gp, m=m):
                                    if m + 2 < NM:
                                        p.wait(gp, m_dmm[m])
                                        load_wd(gp, m + 2)
                            else:
                                @blk.gpsimd
                                def _(gp, m=m):
                                    v.wait(gp, m_dev[m])
                                    m_accw[m] = ga.issue(gp.dma_start(
                                        acc_sp[m * 128:(m + 1) * 128, :],
                                        dtmp[m % 2][:],
                                        accum_op=ALU.add), m)
                                    if m + 2 < NM:
                                        load_wd(gp, m + 2)
                        else:
                            @blk.vector
                            def _(vec, m=m):
                                p.wait(vec, m_dmm[m])
                                ye.wait(vec, m_ep[m])
                                if m >= 2:
                                    a.wait(vec, m_gt[m - 2])
                                m_t2[m] = v.inc(vec.tensor_add(
                                    t2[m % 2][:], pd[m % 2][:],
                                    acc_in[m % 2][:]))

                            @blk.scalar
                            def _(sc, m=m):
                                v.wait(sc, m_t2[m])
                                if m >= 2:
                                    v.wait(sc, m_vout[m - 2])
                                m_gt[m] = a.inc(sc.activation(
                                    gt[m % 2][:], t2[m % 2][:], AF.Silu,
                                    scale=ds_sb[:, m:m + 1]))

                            @blk.vector
                            def _(vec, m=m):
                                a.wait(vec, m_gt[m])
                                if m >= 2:
                                    yw.wait(vec, m_outw[m - 2])
                                i6 = v.inc(vec.tensor_mul(
                                    sg[:], s_in[m % 2][:], gt[m % 2][:]))
                                v.wait(vec, i6)
                                m_vout[m] = v.inc(vec.tensor_add(
                                    out_t[m % 2][:], sg[:], x_in[m % 2][:]))

                            @blk.sync
                            def _(sync, m=m):
                                v.wait(sync, m_vout[m])
                                m_outw[m] = yw.issue(sync.dma_start(
                                    out_fm[m * 128:(m + 1) * 128, :],
                                    out_t[m % 2][:]), m)
                                if m + 2 < NM:
                                    ye.issue(sync.dma_start(
                                        acc_in[m % 2][:],
                                        acc_sp[(m + 2) * 128:(m + 3) * 128, :]),
                                        m + 2)
                                    ye.issue(sync.dma_start(
                                        s_in[m % 2][:],
                                        s_sp[(m + 2) * 128:(m + 3) * 128, :]),
                                        m + 2)
                                    m_ep[m + 2] = ye.issue(sync.dma_start(
                                        x_in[m % 2][:],
                                        x_fm[(m + 2) * 128:(m + 3) * 128, :]),
                                        m + 2)

                            @blk.gpsimd
                            def _(gp, m=m):
                                if m + 2 < NM:
                                    p.wait(gp, m_dmm[m])
                                    load_wd(gp, m + 2)

                    @blk.sync
                    def _(sync):
                        if fgi == 0 or last:
                            yw.wait_all(sync)
                    if fgi == 1:
                        @blk.gpsimd
                        def _(gp):
                            ga.wait_all(gp)

                last_dn_mark[0] = m_dmm[NM - 1]

        with ExitStack() as esFg0:
            h_sb = esFg0.enter_context(
                nc.sbuf_tensor("h_sb0", [128, 29, TOWN], BF16))
            emit_gu(0, xn_t1, None)
            emit_dn(0)
        xn1_es.close()

        with ExitStack() as esA:
            sbA = lambda n, s_, d: esA.enter_context(nc.sbuf_tensor(n, s_, d))
            v_tm = sbA("v_tm", [128, 2, 8, TOWN], BF16)
            attnT = sbA("attnT", [128, 16, TOWN], BF16)
            k_es = ExitStack()
            k_fm = k_es.enter_context(
                nc.sbuf_tensor("k_fm", [128, 2, ND, TOWN], BF16))
            attn_es = ExitStack()
            attn_bf = attn_es.enter_context(
                nc.sbuf_tensor("attn_bf", [128, 8, S], BF16))
            m_kv = [None] * 4
            NQ = TOWN // 128
            m_nrm = [0] * NQ

            with ExitStack() as esS:
                sbs = lambda n, s_, d: esS.enter_context(nc.sbuf_tensor(n, s_, d))
                pss = lambda n, s_, d: esS.enter_context(nc.psum_tensor(n, s_, d))
                negmax = [sbs(f"negmax{i}", [128, 1], F32) for i in range(2)]
                sum_r = [sbs(f"sum_r{i}", [128, 1], F32) for i in range(2)]
                rsum = [sbs(f"rsum{i}", [128, 1], F32) for i in range(2)]
                sc_ps = [pss(f"sc_ps{i}", [128, S], F32) for i in range(2)]

                m_scmm = [0] * NQ
                m_max = [0] * NQ; m_exp = [0] * NQ

                with nc.Block(no_gpsimd_drain=True) as blk:
                    @blk.scalar
                    def _(sc):
                        sc.preload_act_table(AF.Exp)
                        co.wait(sc, 1)
                        pid = sc.partition_id()
                        base = (pid - (pid % 2)) * BLOB
                        pat = [[ND * TOWN, 128], [1, ND * TOWN]]
                        m_kv[1] = zA.issue(sc.dma_start(
                            k_fm[:, 1],
                            bass.AP(kv_all.tensor, base + BLOB, pat)), 0)
                        m_kv[3] = zA.issue(sc.dma_start(
                            v_tm[:, 1],
                            bass.AP(kv_all.tensor, base + BLOB + KN, pat)), 1)

                    @blk.sync
                    def _(sync):
                        co.wait(sync, 1)
                        pid = sync.partition_id()
                        base = (pid - (pid % 2)) * BLOB
                        pat = [[ND * TOWN, 128], [1, ND * TOWN]]
                        m_kv[0] = yA.issue(sync.dma_start(
                            k_fm[:, 0],
                            bass.AP(kv_all.tensor, base, pat)), 0)
                        m_kv[2] = yA.issue(sync.dma_start(
                            v_tm[:, 0],
                            bass.AP(kv_all.tensor, base + KN, pat)), 1)

                    for qc in range(NQ):
                        @blk.tensor
                        def _(te, qc=qc):
                            if qc >= 2:
                                a.wait(te, m_exp[qc - 2])
                            for ks in range(4):
                                if qc == 0:
                                    (yA if ks < 2 else zA).wait(
                                        te, m_kv[ks // 2])
                                for jd in range(ND):
                                    mm = te.matmul(
                                        sc_ps[qc % 2][:, ks * 512:(ks + 1) * 512],
                                        q_sb[:, jd, qc * 128:(qc + 1) * 128],
                                        k_fm[:, ks // 2, jd,
                                             (ks % 2) * 512:(ks % 2) * 512 + 512],
                                        start=(jd == 0), stop=(jd == ND - 1))
                            m_scmm[qc] = p.inc(mm)

                        @blk.vector
                        def _(vec, qc=qc):
                            p.wait(vec, m_scmm[qc])
                            if qc >= 2:
                                a.wait(vec, m_exp[qc - 2])
                            m_max[qc] = v.inc(vec.tensor_reduce(
                                negmax[qc % 2][:], sc_ps[qc % 2][:],
                                AX.X, ALU.max, negate=True))

                        @blk.scalar
                        def _(sc, qc=qc):
                            v.wait(sc, m_max[qc])
                            m_exp[qc] = a.inc(sc.activation(
                                attn_bf[:, qc, :], sc_ps[qc % 2][:], AF.Exp,
                                bias=negmax[qc % 2][:, 0:1],
                                accum_out=sum_r[qc % 2][:, 0:1]))

                        @blk.vector
                        def _(vec, qc=qc):
                            a.wait(vec, m_exp[qc])
                            i5 = v.inc(vec.reciprocal(rsum[qc % 2][:, 0:1],
                                                      sum_r[qc % 2][:, 0:1]))
                            v.wait(vec, i5)
                            m_nrm[qc] = v.inc(vec.tensor_scalar(
                                attn_bf[:, qc, :], attn_bf[:, qc, :],
                                rsum[qc % 2][:, 0:1], None, ALU.mult))

            with ExitStack() as esT:
                psT = lambda n, s_, d: esT.enter_context(nc.psum_tensor(n, s_, d))
                tr2_ps = [psT(f"tr2ps{i}", [128, 128], BF16) for i in range(4)]
                m_tr2 = []; m_tcp = []

                with nc.Block(no_gpsimd_drain=True) as blk:
                    for qc in range(NQ):
                        for kb in range(16):
                            ti = qc * 16 + kb

                            @blk.tensor
                            def _(te, qc=qc, kb=kb, ti=ti):
                                if kb == 0:
                                    v.wait(te, m_nrm[qc])
                                if ti >= 4:
                                    a.wait(te, m_tcp[ti - 4])
                                m_tr2.append(p.inc(te.transpose(
                                    tr2_ps[ti % 4][:],
                                    attn_bf[:, qc, kb * 128:(kb + 1) * 128],
                                    id_sb[:])))

                            @blk.scalar
                            def _(sc, qc=qc, kb=kb, ti=ti):
                                p.wait(sc, m_tr2[ti])
                                m_tcp.append(a.inc(sc.activation(
                                    attnT[:, kb, qc * 128:(qc + 1) * 128],
                                    tr2_ps[ti % 4][:], AF.Copy)))
            attn_es.close()
            k_es.close()
            q_es.close()

            xn2_es = ExitStack()
            xn_own2 = xn2_es.enter_context(
                nc.sbuf_tensor("xn_own2", [128, NA, TOWN], BF16, side="right"))
            m_xn4 = [None] * 4
            pv_es = ExitStack()
            pv_fm = pv_es.enter_context(
                nc.sbuf_tensor("pv_fm", [128, ND, TOWN], BF16, side="right"))
            wo_es = ExitStack()
            wo_buf = [wo_es.enter_context(
                nc.sbuf_tensor(f"wob{i}", [128, ND, 128], BF16, side="right"))
                for i in range(4)]
            m_wo = [None] * NM

            with ExitStack() as esP:
                psp = lambda n, s_, d: esP.enter_context(nc.psum_tensor(n, s_, d))
                pv_ps = [psp(f"pvps{i}", [128, TOWN], F32) for i in range(2)]
                m_pvmm = [0] * ND; m_pvev = [0] * ND

                with nc.Block(no_gpsimd_drain=True) as blk:
                    @blk.gpsimd
                    def _(gp):
                        for mq in range(4):
                            m_wo[mq] = gwo.issue(
                                gp.dma_start(wo_buf[mq][:], wo_st[mq]), mq)

                    @blk.sync
                    def _(sync):
                        yn.wait_all(sync)
                        m_xn4[0] = yA.issue(sync.dma_start(
                            xn_own2[:, 0:8, :], xn_q0[:, 0:8, :]), 0)
                        m_xn4[2] = yA.issue(sync.dma_start(
                            xn_own2[:, 16:24, :], xn_q0[:, 16:24, :]), 1)

                    @blk.scalar
                    def _(sc):
                        m_xn4[1] = zA.issue(sc.dma_start(
                            xn_own2[:, 8:16, :], xn_q0[:, 8:16, :]), 0)
                        m_xn4[3] = zA.issue(sc.dma_start(
                            xn_own2[:, 24:32, :], xn_q0[:, 24:32, :]), 1)

                    for dc in range(ND):
                        @blk.tensor
                        def _(te, dc=dc):
                            if dc == 0:
                                yA.wait(te, m_kv[2])
                                zA.wait(te, m_kv[3])
                            if dc >= 2:
                                v.wait(te, m_pvev[dc - 2])
                            for ks in (0, 512):
                                for kc in range(16):
                                    mm = te.matmul(
                                        pv_ps[dc % 2][:, ks:ks + 512],
                                        v_tm[:, kc // 8, kc % 8,
                                             dc * 128:(dc + 1) * 128],
                                        attnT[:, kc, ks:ks + 512],
                                        start=(kc == 0), stop=(kc == 15))
                            m_pvmm[dc] = p.inc(mm)

                        @blk.vector
                        def _(vec, dc=dc):
                            p.wait(vec, m_pvmm[dc])
                            m_pvev[dc] = v.inc(vec.tensor_copy(
                                pv_fm[:, dc, :], pv_ps[dc % 2][:]))

        with ExitStack() as esO:
            sbo = lambda n, s_, d: esO.enter_context(nc.sbuf_tensor(n, s_, d))
            pso = lambda n, s_, d: esO.enter_context(nc.psum_tensor(n, s_, d))
            s_t = [sbo(f"st{i}", [128, TOWN], BF16) for i in range(2)]
            sp_ps = [pso(f"spps{i}", [128, TOWN], F32) for i in range(2)]
            m_spmm = [0] * NM; m_spev = [0] * NM; m_sw = [None] * NM

            with nc.Block(no_gpsimd_drain=True) as blk:
                for m in range(NM):
                    if m >= 4:
                        @blk.gpsimd
                        def _(gp, m=m):
                            p.wait(gp, m_spmm[m - 4])
                            gwo.wait(gp, m_wo[m - 4])
                            m_wo[m] = gwo.issue(gp.dma_start(
                                wo_buf[m % 4][:], wo_st[m]), m)

                    @blk.tensor
                    def _(te, m=m):
                        gwo.wait(te, m_wo[m])
                        if m >= 2:
                            v.wait(te, m_spev[m - 2])
                        for ks in (0, 512):
                            for dc in range(ND):
                                mm = te.matmul(
                                    sp_ps[m % 2][:, ks:ks + 512],
                                    wo_buf[m % 4][:, dc, :],
                                    pv_fm[:, dc, ks:ks + 512],
                                    start=(dc == 0), stop=(dc == ND - 1))
                        m_spmm[m] = p.inc(mm)

                    @blk.vector
                    def _(vec, m=m):
                        p.wait(vec, m_spmm[m])
                        if m >= 2:
                            yw.wait(vec, m_sw[m - 2])
                        m_spev[m] = v.inc(vec.tensor_scalar(
                            s_t[m % 2][:], sp_ps[m % 2][:],
                            bo_sb[:, m:m + 1], None, ALU.add))

                    @blk.sync
                    def _(sync, m=m):
                        v.wait(sync, m_spev[m])
                        m_sw[m] = yw.issue(sync.dma_start(
                            s_sp[m * 128:(m + 1) * 128, :], s_t[m % 2][:]), m)

                @blk.sync
                def _(sync):
                    yw.wait_all(sync)

        wo_es.close()
        pv_es.close()

        with ExitStack() as esFg1:
            h_sb = esFg1.enter_context(
                nc.sbuf_tensor("h_sb1", [128, 29, TOWN], BF16))
            emit_gu(1, xn_own2, m_xn4)
            emit_dn(1)

        with ExitStack() as esFg2:
            h_sb = esFg2.enter_context(
                nc.sbuf_tensor("h_sb2", [128, 29, TOWN], BF16))
            emit_gu(2, xn_own2, m_xn4)
            xn2_es.close()
            emit_dn(2)

        esH.close()

    return nc





def _stage_weights(Wqkv, bqkv, Wo, bo, gate_w, gate_s, up_w, up_s, down_w, down_s):
    st = {}
    st["wqkv_st"] = np.ascontiguousarray(
        np.asarray(Wqkv, np.float32).reshape(NA, 128, NJ, 128).transpose(2, 1, 0, 3))
    st["bqkv_st"] = np.ascontiguousarray(
        np.asarray(bqkv, np.float32).reshape(NJ, 128).T)
    st["wo_st"] = np.ascontiguousarray(
        np.asarray(Wo, np.float32).reshape(ND, 128, NM, 128).transpose(2, 1, 0, 3))
    st["bo_st"] = np.ascontiguousarray(
        np.asarray(bo, np.float32).reshape(NM, 128).T)
    st["wg_st"] = np.ascontiguousarray(
        np.asarray(gate_w, np.int32).reshape(NF, 128, NA, 128).transpose(0, 3, 2, 1))
    st["wu_st"] = np.ascontiguousarray(
        np.asarray(up_w, np.int32).reshape(NF, 128, NA, 128).transpose(0, 3, 2, 1))
    st["wd_st"] = np.ascontiguousarray(
        np.asarray(down_w, np.int32).reshape(NM, 128, NF, 128).transpose(0, 3, 2, 1))
    st["gs_st"] = np.ascontiguousarray(
        np.asarray(gate_s, np.float32).reshape(NF, 128).T)
    st["us_st"] = np.ascontiguousarray(
        np.asarray(up_s, np.float32).reshape(NF, 128).T)
    st["ds_st"] = np.ascontiguousarray(
        np.asarray(down_s, np.float32).reshape(NM, 128).T)
    st["ident"] = np.eye(128, dtype=np.float32).astype(bf16)
    return st


def kernel(L0_attn_x, L0_attn_Wqkv, L0_attn_bqkv, L0_attn_Wo, L0_attn_bo,
           L0_ffn_gate_w, L0_ffn_gate_s, L0_ffn_up_w, L0_ffn_up_s,
           L0_ffn_down_w, L0_ffn_down_s):
    x = np.asarray(L0_attn_x, np.float32)
    B, Sq, Cm = x.shape
    assert (B, Sq, Cm) == (4, S, C)

    shared = _stage_weights(L0_attn_Wqkv, L0_attn_bqkv, L0_attn_Wo, L0_attn_bo,
                            L0_ffn_gate_w, L0_ffn_gate_s, L0_ffn_up_w,
                            L0_ffn_up_s, L0_ffn_down_w, L0_ffn_down_s)

    in_maps = []
    for core in range(8):
        b, hh = core // 2, core % 2
        m = dict(shared)
        slab = x[b, hh * TOWN:(hh + 1) * TOWN, :]
        m["x_fm"] = np.ascontiguousarray(slab.T)
        m["x_st"] = np.ascontiguousarray(
            slab.reshape(NLT, LT, NA, 128).transpose(0, 3, 2, 1))
        in_maps.append(m)

    nc = build()
    res = run_bass_kernel_spmd(nc, in_maps, core_ids=list(range(8)))

    out = np.empty((B, S, C), np.float32)
    for core in range(8):
        b, hh = core // 2, core % 2
        out[b, hh * TOWN:(hh + 1) * TOWN, :] = res.results[core]["out_fm"].T
    return out


# version 20
# speedup vs baseline: 1.0099x; 1.0099x over previous
import numpy as np
import ml_dtypes
from contextlib import ExitStack

import concourse.bass as bass
import concourse.mybir as mybir
from concourse.bass_utils import run_bass_kernel_spmd

F32 = mybir.dt.float32
BF16 = mybir.dt.bfloat16
I32 = mybir.dt.int32
AF = mybir.ActivationFunctionType
ALU = mybir.AluOpType
AX = mybir.AxisListType
bf16 = ml_dtypes.bfloat16

C = 4096
S = 2048
TOWN = 1024
NA = 32
NJ = 24
ND = 8
NF = 86
NM = 32
LT = 128
NLT = TOWN // LT
FGROUPS = [(0, 29), (29, 29), (58, 28)]
EPS = 1e-5
SCALE = 1.0 / 32.0
KN = 128 * ND * TOWN
BLOB = 2 * KN


class Ctr:
    def __init__(self, nc, es, name):
        self.h = es.enter_context(nc.semaphore(name))
        self.n = 0

    def inc(self, instr, k=1):
        instr.then_inc(self.h, k)
        self.n += k
        return self.n

    def wait(self, eng, val):
        if val > 0:
            eng.wait_ge(self.h, val)


class DmaStream:
    def __init__(self, nc, es, name, depth=2):
        self.sems = [es.enter_context(nc.semaphore(f"{name}{i}"))
                     for i in range(depth)]
        self.cnt = [0] * depth
        self.depth = depth

    def issue(self, instr, k=0):
        i = k % self.depth
        instr.then_inc(self.sems[i], 16)
        self.cnt[i] += 16
        return (i, self.cnt[i])

    def wait(self, eng, mark):
        if mark:
            eng.wait_ge(self.sems[mark[0]], mark[1])

    def wait_all(self, eng):
        for i in range(self.depth):
            if self.cnt[i]:
                eng.wait_ge(self.sems[i], self.cnt[i])


def build(debug=False):
    nc = bass.Bass("TRN2", target_bir_lowering=False)

    x_fm = nc.dram_tensor("x_fm", [C, TOWN], F32, kind="ExternalInput").ap()
    x_st = nc.dram_tensor("x_st", [TOWN // LT, 128, NA, LT], F32,
                          kind="ExternalInput").ap()
    wqkv_st = nc.dram_tensor("wqkv_st", [NJ, 128, NA, 128], F32, kind="ExternalInput").ap()
    bqkv_st = nc.dram_tensor("bqkv_st", [128, NJ], F32, kind="ExternalInput").ap()
    wo_st = nc.dram_tensor("wo_st", [NM, 128, ND, 128], F32, kind="ExternalInput").ap()
    bo_st = nc.dram_tensor("bo_st", [128, NM], F32, kind="ExternalInput").ap()
    wg_st = nc.dram_tensor("wg_st", [NF, 128, NA, 128], I32, kind="ExternalInput").ap()
    wu_st = nc.dram_tensor("wu_st", [NF, 128, NA, 128], I32, kind="ExternalInput").ap()
    wd_st = nc.dram_tensor("wd_st", [NM, 128, NF, 128], I32, kind="ExternalInput").ap()
    gs_st = nc.dram_tensor("gs_st", [128, NF], F32, kind="ExternalInput").ap()
    us_st = nc.dram_tensor("us_st", [128, NF], F32, kind="ExternalInput").ap()
    ds_st = nc.dram_tensor("ds_st", [128, NM], F32, kind="ExternalInput").ap()
    ident = nc.dram_tensor("ident", [128, 128], BF16, kind="ExternalInput").ap()

    out_fm = nc.dram_tensor("out_fm", [C, TOWN], F32, kind="ExternalOutput").ap()

    scr = "ExternalOutput" if debug else "Internal"
    xn_sp = nc.dram_tensor("xn_sp", [C, TOWN], BF16, kind=scr).ap()
    s_sp = nc.dram_tensor("s_sp", [C, TOWN], BF16, kind=scr).ap()
    acc_sp = nc.dram_tensor("acc_sp", [C, TOWN], F32, kind=scr).ap()
    kv_loc = nc.dram_tensor("kv_loc", [1, BLOB], BF16, kind="Internal").ap()
    kv_all = nc.dram_tensor("kv_all", [8, BLOB], BF16, kind="Internal",
                            addr_space="Shared").ap()
    kvl_r = kv_loc.rearrange("r (s p j t) -> (r s) p j t", s=2, p=128, j=ND)
    xn_q0 = xn_sp.rearrange("(a p) t -> p a t", p=128)

    with ExitStack() as es0:
        sb0 = lambda n, s_, d: es0.enter_context(nc.sbuf_tensor(n, s_, d))
        bqkv_sb = sb0("bqkv_sb", [128, NJ], F32)
        bo_sb = sb0("bo_sb", [128, NM], F32)
        gs_sb = sb0("gs_sb", [128, NF], F32)
        us_sb = sb0("us_sb", [128, NF], F32)
        ds_sb = sb0("ds_sb", [128, NM], F32)
        id_sb = sb0("id_sb", [128, 128], BF16)
        onesm_sb = sb0("onesm_sb", [128, 128], BF16)
        eps_sb = sb0("eps_sb", [128, 1], F32)

        p = Ctr(nc, es0, "p")
        a = Ctr(nc, es0, "a")
        v = Ctr(nc, es0, "v")
        co = Ctr(nc, es0, "co")
        ys = DmaStream(nc, es0, "ys", 1)
        yx = DmaStream(nc, es0, "yx", 4)
        yn = DmaStream(nc, es0, "yn", 2)
        yw = DmaStream(nc, es0, "yw", 2)
        ye = DmaStream(nc, es0, "ye", 2)
        yk = DmaStream(nc, es0, "yk", 1)
        zs = DmaStream(nc, es0, "zs", 1)
        yA = DmaStream(nc, es0, "yA", 2)
        zA = DmaStream(nc, es0, "zA", 2)
        gw = DmaStream(nc, es0, "gw", 2)
        gwo = DmaStream(nc, es0, "gwo", 4)
        gd = DmaStream(nc, es0, "gd", 2)
        ga = DmaStream(nc, es0, "ga", 2)

        q_es = ExitStack()
        q_sb = q_es.enter_context(
            nc.sbuf_tensor("q_sb", [128, ND, TOWN], BF16, side="right"))
        xn1_es = ExitStack()
        xn_t1 = xn1_es.enter_context(
            nc.sbuf_tensor("xn_t1", [128, NA, TOWN], BF16, side="right"))
        wq_es = ExitStack()
        wq_buf = [wq_es.enter_context(
            nc.sbuf_tensor(f"wqb{i}", [128, NA, 128], BF16, side="right"))
            for i in range(3)]

        lnb_es = ExitStack()
        xsq = [lnb_es.enter_context(
            nc.sbuf_tensor(f"xsq{i}", [128, NA, LT], BF16))
            for i in range(NLT)]
        m_xld = [None] * NLT
        m_wqE = [None] * ND
        m_qkvE = [0] * ND; m_qevE = [0] * ND

        with nc.Block(no_gpsimd_drain=True) as blk:
            @blk.sync
            def _(sync):
                for dst, src in ((bqkv_sb, bqkv_st), (bo_sb, bo_st),
                                 (gs_sb, gs_st), (us_sb, us_st),
                                 (ds_sb, ds_st), (id_sb, ident)):
                    ys.issue(sync.dma_start(dst[:], src))
                ys.wait_all(sync)

            @blk.vector
            def _(vec):
                vec.memset(onesm_sb[:], 1.0)
                vec.memset(eps_sb[:], EPS)

            @blk.gpsimd
            def _(gp):
                for t0 in range(4):
                    m_xld[t0] = yx.issue(
                        gp.dma_start(xsq[t0][:], x_st[t0]), t0)
                m_wqE[0] = gw.issue(
                    gp.dma_start(wq_buf[0][:], wqkv_st[0]), 0)

        with ExitStack() as es1:
            sb = lambda n, s_, d: es1.enter_context(nc.sbuf_tensor(n, s_, d))
            ps = lambda n, s_, d: es1.enter_context(nc.psum_tensor(n, s_, d))
            sqb = [sb(f"sqb{i}", [128, NA, LT], BF16) for i in range(2)]
            tmp_f = sb("tmp_f", [128, NA, LT], BF16)
            mm_r = [sb(f"mm{i}", [128, 2 * LT], F32) for i in range(4)]
            nm2_r = [sb(f"nm2_{i}", [128, LT], F32) for i in range(4)]
            ab_r = [sb(f"abr{i}", [128, 2 * LT], F32) for i in range(4)]
            stat_ps = [ps(f"statps{i}", [128, 2 * LT], F32) for i in range(4)]
            psE = [ps(f"psE{i}", [128, 512], F32) for i in range(2)]

            xn_r = xn_sp.rearrange("(a p) (n t) -> n p a t", p=128, t=LT)

            m_xnw = [None] * NLT
            m_sq = [0] * NLT
            m_st = [0] * NLT; m_vm = [0] * NLT; m_sqrt = [0] * NLT
            m_ab = [0] * NLT; m_bc = [0] * NLT; m_abe = [0] * NLT
            m_apm = [0] * NLT; m_ap = [0] * NLT

            with nc.Block(no_gpsimd_drain=True) as blk:
                @blk.scalar
                def _(sc):
                    sc.preload_act_table(AF.Sqrt)

                def emit_front(t):
                    tb = t % 4

                    @blk.scalar
                    def _(sc, t=t):
                        yx.wait(sc, m_xld[t])
                        if t >= 2:
                            p.wait(sc, m_st[t - 2])
                        m_sq[t] = a.inc(sc.activation(
                            sqb[t % 2][:], xsq[t][:], AF.Square))

                    @blk.tensor
                    def _(te, t=t, tb=tb):
                        yx.wait(te, m_xld[t])
                        if t >= 4:
                            v.wait(te, m_vm[t - 4])
                        for ca in range(NA):
                            mm = te.matmul(stat_ps[tb][:, 0:LT], onesm_sb[:],
                                           xsq[t][:, ca, :],
                                           start=(ca == 0), stop=(ca == NA - 1))
                        a.wait(te, m_sq[t])
                        for ca in range(NA):
                            mm = te.matmul(stat_ps[tb][:, LT:2 * LT],
                                           onesm_sb[:], sqb[t % 2][:, ca, :],
                                           start=(ca == 0), stop=(ca == NA - 1))
                        m_st[t] = p.inc(mm)

                    @blk.vector
                    def _(vec, t=t, tb=tb):
                        p.wait(vec, m_st[t])
                        if t >= 4:
                            a.wait(vec, m_sqrt[t - 4])
                            v.wait(vec, m_ab[t - 4])
                        i1 = v.inc(vec.tensor_scalar(
                            mm_r[tb][:], stat_ps[tb][:],
                            1.0 / C, None, ALU.mult))
                        v.wait(vec, i1)
                        i2 = v.inc(vec.scalar_tensor_tensor(
                            nm2_r[tb][:], mm_r[tb][:, 0:LT], -1.0,
                            mm_r[tb][:, 0:LT], ALU.mult, ALU.mult))
                        v.wait(vec, i2)
                        m_vm[t] = v.inc(vec.tensor_add(
                            nm2_r[tb][:], mm_r[tb][:, LT:2 * LT],
                            nm2_r[tb][:]))

                def emit_back(t):
                    tb = t % 4

                    @blk.scalar
                    def _(sc, t=t, tb=tb):
                        v.wait(sc, m_vm[t])
                        m_sqrt[t] = a.inc(sc.activation(
                            nm2_r[tb][:], nm2_r[tb][:], AF.Sqrt,
                            bias=eps_sb[:, 0:1]))

                    @blk.vector
                    def _(vec, t=t, tb=tb):
                        a.wait(vec, m_sqrt[t])
                        if t >= 4:
                            v.wait(vec, m_ap[t - 4])
                        i3 = v.inc(vec.reciprocal(ab_r[tb][:, 0:LT],
                                                  nm2_r[tb][:]))
                        v.wait(vec, i3)
                        m_ab[t] = v.inc(vec.scalar_tensor_tensor(
                            ab_r[tb][:, LT:2 * LT], mm_r[tb][:, 0:LT], -1.0,
                            ab_r[tb][:, 0:LT], ALU.mult, ALU.mult))

                    @blk.vector
                    def _(vec, t=t, tb=tb):
                        v.wait(vec, m_ab[t])
                        m_apm[t] = v.inc(vec.tensor_tensor(
                            tmp_f[:], xsq[t][:],
                            ab_r[tb][:, 0:LT].unsqueeze(1)
                            .broadcast_to([128, NA, LT]), ALU.mult))
                        v.wait(vec, m_apm[t])
                        m_ap[t] = v.inc(vec.tensor_tensor(
                            xn_t1[:, :, t * LT:(t + 1) * LT], tmp_f[:],
                            ab_r[tb][:, LT:2 * LT].unsqueeze(1)
                            .broadcast_to([128, NA, LT]), ALU.add))

                    @blk.sync
                    def _(sync, t=t):
                        v.wait(sync, m_ap[t])
                        m_xnw[t] = yn.issue(sync.dma_start(
                            xn_r[t], xn_t1[:, :, t * LT:(t + 1) * LT]), t)

                    @blk.gpsimd
                    def _(gp, t=t):
                        a.wait(gp, m_sq[t])
                        if t + 4 < NLT:
                            m_xld[t + 4] = yx.issue(gp.dma_start(
                                xsq[t + 4][:], x_st[t + 4]), t + 4)
                        for je in ({0: (1,), 1: (2,)}.get(t, ())):
                            m_wqE[je] = gw.issue(gp.dma_start(
                                wq_buf[je % 3][:], wqkv_st[je]), je)

                def emit_qkv_early(je):
                    @blk.tensor
                    def _(te, je=je):
                        gw.wait(te, m_wqE[je])
                        v.wait(te, m_ap[3])
                        if je >= 2:
                            v.wait(te, m_qevE[je - 2])
                        for ca in range(NA):
                            mm = te.matmul(
                                psE[je % 2][:],
                                wq_buf[je % 3][:, ca, :],
                                xn_t1[:, ca, 0:512],
                                start=(ca == 0), stop=(ca == NA - 1))
                        m_qkvE[je] = p.inc(mm)

                    @blk.vector
                    def _(vec, je=je):
                        p.wait(vec, m_qkvE[je])
                        m_qevE[je] = v.inc(vec.tensor_scalar(
                            q_sb[:, je, 0:512], psE[je % 2][:],
                            bqkv_sb[:, je:je + 1], SCALE,
                            ALU.add, ALU.mult))

                    if je + 3 < ND:
                        @blk.gpsimd
                        def _(gp, je=je):
                            p.wait(gp, m_qkvE[je])
                            m_wqE[je + 3] = gw.issue(gp.dma_start(
                                wq_buf[(je + 3) % 3][:], wqkv_st[je + 3]),
                                je + 3)

                for t in range(NLT):
                    emit_front(t)
                    if t >= 1:
                        emit_back(t - 1)
                    if t == 6:
                        emit_qkv_early(0)
                        emit_qkv_early(1)
                    elif t == 7:
                        emit_qkv_early(2)
                        emit_qkv_early(3)
                emit_back(NLT - 1)
                for je in range(4, ND):
                    emit_qkv_early(je)
        lnb_es.close()

        esH = ExitStack()
        sbh = lambda n, s_, d: esH.enter_context(nc.sbuf_tensor(n, s_, d))
        wgb = [sbh(f"wgb{i}", [128, NA, 128], BF16) for i in range(2)]
        wub = [sbh(f"wub{i}", [128, NA, 128], BF16) for i in range(2)]
        wdb = [sbh(f"wdb{i}", [128, 29, 128], BF16) for i in range(2)]
        m_wgu_all = [[None] * na for (a0, na) in FGROUPS]
        m_wd_all = [[None] * NM for _ in FGROUPS]
        last_gu_mark = [0]
        last_dn_mark = [0]
        m_xno2 = [None]


        with ExitStack() as esQ:
            sbq = lambda n, s_, d: esQ.enter_context(nc.sbuf_tensor(n, s_, d))
            psq = lambda n, s_, d: esQ.enter_context(nc.psum_tensor(n, s_, d))
            k_own = sbq("k_own", [128, ND, TOWN], BF16)
            v_stage = sbq("v_stage", [128, ND, TOWN], BF16)
            v_tmo = sbq("v_tmo", [128, ND, TOWN], BF16)
            ps_qkv = [psq(f"psqkv{i}", [128, TOWN], F32) for i in range(2)]
            tr_ps = [psq(f"trps{i}", [128, 128], BF16) for i in range(2)]

            m_wq = [None] * NJ
            m_qmm = [0] * NJ; m_qev = [0] * NJ
            m_vev = [0] * ND
            m_vtr = []; m_vcp = []

            with nc.Block(no_gpsimd_drain=True) as blk:
                for i in range(NJ):
                    j = i

                    @blk.gpsimd
                    def _(gp, i=i, j=j):
                        if i < 3:
                            p.wait(gp, m_qkvE[5 + i])
                        else:
                            p.wait(gp, m_qmm[i - 3])
                        m_wq[i] = gw.issue(gp.dma_start(
                            wq_buf[(ND + i) % 3][:], wqkv_st[j]), ND + i)

                    @blk.tensor
                    def _(te, i=i):
                        gw.wait(te, m_wq[i])
                        if i >= 2:
                            v.wait(te, m_qev[i - 2])
                        for ks in ((512,) if i < ND else (0, 512)):
                            for ca in range(NA):
                                mm = te.matmul(
                                    ps_qkv[i % 2][:, ks:ks + 512],
                                    wq_buf[(ND + i) % 3][:, ca, :],
                                    xn_t1[:, ca, ks:ks + 512],
                                    start=(ca == 0), stop=(ca == NA - 1))
                        m_qmm[i] = p.inc(mm)

                    @blk.vector
                    def _(vec, i=i, j=j):
                        p.wait(vec, m_qmm[i])
                        if j < 8:
                            m_qev[i] = v.inc(vec.tensor_scalar(
                                q_sb[:, j, 512:1024],
                                ps_qkv[i % 2][:, 512:1024],
                                bqkv_sb[:, j:j + 1], SCALE,
                                ALU.add, ALU.mult))
                        else:
                            dst = (k_own[:, j - 8, :] if j < 16
                                   else v_stage[:, j - 16, :])
                            m_qev[i] = v.inc(vec.tensor_scalar(
                                dst, ps_qkv[i % 2][:],
                                bqkv_sb[:, j:j + 1], None, ALU.add))
                        if j >= 16:
                            m_vev[j - 16] = m_qev[i]

                    if i == NJ - 1:
                        for jd in range(ND):
                            for kb in range(ND):
                                ti = jd * ND + kb

                                @blk.tensor
                                def _(te, jd=jd, kb=kb, ti=ti):
                                    v.wait(te, m_vev[jd])
                                    if ti >= 2:
                                        v.wait(te, m_vcp[ti - 2])
                                    m_vtr.append(p.inc(te.transpose(
                                        tr_ps[ti % 2][:],
                                        v_stage[:, jd, kb * 128:(kb + 1) * 128],
                                        id_sb[:])))

                                @blk.vector
                                def _(vec, jd=jd, kb=kb, ti=ti):
                                    p.wait(vec, m_vtr[ti])
                                    m_vcp.append(v.inc(vec.tensor_copy(
                                        v_tmo[:, kb, jd * 128:(jd + 1) * 128],
                                        tr_ps[ti % 2][:])))

                @blk.sync
                def _(sync):
                    for i8 in range(8, 16):
                        v.wait(sync, m_qev[i8])
                    yk.issue(sync.dma_start(kvl_r[0], k_own[:]))
                    v.wait(sync, v.n)
                    yk.issue(sync.dma_start(kvl_r[1], v_tmo[:]))
                    yk.wait_all(sync)
                    yn.wait_all(sync)

                @blk.gpsimd
                def _(gp):
                    p.wait(gp, m_qmm[NJ - 6])
                    j0 = FGROUPS[0][0]
                    gw.issue(gp.dma_start(wgb[0][:], wg_st[j0]), 0)
                    m_wgu_all[0][0] = gw.issue(gp.dma_start(wub[0][:],
                                                            wu_st[j0]), 0)
                    gw.issue(gp.dma_start(wgb[1][:], wg_st[j0 + 1]), 1)
                    m_wgu_all[0][1] = gw.issue(gp.dma_start(wub[1][:],
                                                            wu_st[j0 + 1]), 1)
                    yk.wait_all(gp)
                    co.inc(gp.collective_compute(
                        "AllGather", ALU.bypass,
                        replica_groups=[list(range(8))],
                        ins=[kv_loc.opt()], outs=[kv_all.opt()]))

        wq_es.close()

        def emit_gu(fgi, xn_buf, m_xn_ready):
            a0, na = FGROUPS[fgi]
            m_wgu = m_wgu_all[fgi]
            m_wd = m_wd_all[fgi]

            with ExitStack() as esG:
                sbg = lambda n, s_, d: esG.enter_context(nc.sbuf_tensor(n, s_, d))
                psg = lambda n, s_, d: esG.enter_context(nc.psum_tensor(n, s_, d))
                gtmp = [sbg(f"gtmp{fgi}_{i}", [128, TOWN], F32) for i in range(2)]
                pg = [psg(f"pg{fgi}_{i}", [128, TOWN], F32) for i in range(2)]
                pu = [psg(f"pu{fgi}_{i}", [128, TOWN], F32) for i in range(2)]

                m_gumm = [0] * na; m_sil = [0] * na; m_h = [0] * na

                with nc.Block(no_gpsimd_drain=True) as blk:
                    @blk.scalar
                    def _(sc):
                        sc.preload_act_table(AF.Silu)

                    for jj in range(na):
                        j = a0 + jj

                        if jj >= 2:
                            @blk.gpsimd
                            def _(gp, jj=jj, j=j):
                                p.wait(gp, m_gumm[jj - 2])
                                gw.issue(gp.dma_start(wgb[jj % 2][:],
                                                      wg_st[j]), jj)
                                m_wgu[jj] = gw.issue(gp.dma_start(
                                    wub[jj % 2][:], wu_st[j]), jj)

                        if jj == na - 10:
                            @blk.gpsimd
                            def _(gp, fgi=fgi, na=na, a0=a0):
                                if fgi > 0:
                                    p.wait(gp, last_dn_mark[0])
                                for mq in (0, 1):
                                    m_wd[mq] = gd.issue(gp.dma_start(
                                        wdb[mq][:, 0:na, :],
                                        wd_st[mq][:, a0:a0 + na, :]), mq)

                        @blk.tensor
                        def _(te, jj=jj):
                            gw.wait(te, m_wgu[jj])
                            if jj >= 2:
                                a.wait(te, m_sil[jj - 2])
                                v.wait(te, m_h[jj - 2])
                            for buf, wb in ((pg, wgb), (pu, wub)):
                                for ks in (0, 512):
                                    for ca in range(NA):
                                        if jj == 0 and m_xn_ready \
                                                and ca % 8 == 0 and ks == 0 \
                                                and buf is pg:
                                            (yA if (ca // 8) % 2 == 0 else zA) \
                                                .wait(te, m_xn_ready[ca // 8])
                                        mm = te.matmul(
                                            buf[jj % 2][:, ks:ks + 512],
                                            wb[jj % 2][:, ca, :],
                                            xn_buf[:, ca, ks:ks + 512],
                                            start=(ca == 0), stop=(ca == NA - 1))
                            m_gumm[jj] = p.inc(mm)

                        @blk.scalar
                        def _(sc, jj=jj, j=j):
                            p.wait(sc, m_gumm[jj])
                            if jj >= 2:
                                v.wait(sc, m_h[jj - 2])
                            m_sil[jj] = a.inc(sc.activation(
                                gtmp[jj % 2][:], pg[jj % 2][:], AF.Silu,
                                scale=gs_sb[:, j:j + 1]))

                        @blk.vector
                        def _(vec, jj=jj, j=j):
                            a.wait(vec, m_sil[jj])
                            m_h[jj] = v.inc(vec.scalar_tensor_tensor(
                                h_sb[:, jj, :], pu[jj % 2][:],
                                us_sb[:, j:j + 1], gtmp[jj % 2][:],
                                ALU.mult, ALU.mult))

                last_gu_mark[0] = m_gumm[na - 1]

        def emit_dn(fgi):
            a0, na = FGROUPS[fgi]
            m_wd = m_wd_all[fgi]
            last = fgi == len(FGROUPS) - 1

            with ExitStack() as esD:
                sbd = lambda n, s_, d: esD.enter_context(nc.sbuf_tensor(n, s_, d))
                psd = lambda n, s_, d: esD.enter_context(nc.psum_tensor(n, s_, d))
                pd = [psd(f"pd{fgi}_{i}", [128, TOWN], F32) for i in range(2)]
                m_dmm = [0] * NM

                if not last:
                    dtmp = [sbd(f"dtmp{fgi}_{i}", [128, TOWN], F32)
                            for i in range(2)]
                    m_dev = [0] * NM; m_accw = [None] * NM
                else:
                    acc_in = [sbd(f"acci{i}", [128, TOWN], F32) for i in range(2)]
                    s_in = [sbd(f"sin{i}", [128, TOWN], BF16) for i in range(2)]
                    x_in = [sbd(f"xin{i}", [128, TOWN], F32) for i in range(2)]
                    out_t = [sbd(f"outt{i}", [128, TOWN], F32) for i in range(2)]
                    t2 = [sbd(f"t2{i}", [128, TOWN], F32) for i in range(2)]
                    gt = [sbd(f"gt{i}", [128, TOWN], F32) for i in range(2)]
                    sg = sbd("sg", [128, TOWN], F32)
                    m_t2 = [0] * NM; m_gt = [0] * NM
                    m_vout = [0] * NM; m_outw = [None] * NM
                    m_ep = [None] * NM

                with nc.Block(no_gpsimd_drain=True) as blk:
                    def load_wd(gp, m):
                        m_wd[m] = gd.issue(gp.dma_start(
                            wdb[m % 2][:, 0:na, :],
                            wd_st[m][:, a0:a0 + na, :]), m)

                    if last:
                        @blk.sync
                        def _(sync):
                            ga.wait_all(sync)
                            for m in (0, 1):
                                ye.issue(sync.dma_start(
                                    acc_in[m][:],
                                    acc_sp[m * 128:(m + 1) * 128, :]), m)
                                ye.issue(sync.dma_start(
                                    s_in[m][:],
                                    s_sp[m * 128:(m + 1) * 128, :]), m)
                                m_ep[m] = ye.issue(sync.dma_start(
                                    x_in[m][:],
                                    x_fm[m * 128:(m + 1) * 128, :]), m)

                    for m in range(NM):
                        @blk.tensor
                        def _(te, m=m):
                            gd.wait(te, m_wd[m])
                            if m >= 2:
                                if last:
                                    v.wait(te, m_t2[m - 2])
                                else:
                                    v.wait(te, m_dev[m - 2])
                            for ks in (0, 512):
                                for aa in range(na):
                                    mm = te.matmul(
                                        pd[m % 2][:, ks:ks + 512],
                                        wdb[m % 2][:, aa, :],
                                        h_sb[:, aa, ks:ks + 512],
                                        start=(aa == 0), stop=(aa == na - 1))
                            m_dmm[m] = p.inc(mm)

                        if not last and m == 20:
                            @blk.gpsimd
                            def _(gp, fgi=fgi, a0_next=FGROUPS[fgi + 1][0]):
                                p.wait(gp, last_gu_mark[0])
                                for jq in (0, 1):
                                    gw.issue(gp.dma_start(
                                        wgb[jq][:], wg_st[a0_next + jq]), jq)
                                    m_wgu_all[fgi + 1][jq] = gw.issue(
                                        gp.dma_start(wub[jq][:],
                                                     wu_st[a0_next + jq]), jq)

                        if not last:
                            @blk.vector
                            def _(vec, m=m):
                                p.wait(vec, m_dmm[m])
                                if m >= 2:
                                    if fgi == 0:
                                        yw.wait(vec, m_accw[m - 2])
                                    else:
                                        ga.wait(vec, m_accw[m - 2])
                                m_dev[m] = v.inc(vec.tensor_copy(
                                    dtmp[m % 2][:], pd[m % 2][:]))

                            if fgi == 0:
                                @blk.sync
                                def _(sync, m=m):
                                    v.wait(sync, m_dev[m])
                                    m_accw[m] = yw.issue(sync.dma_start(
                                        acc_sp[m * 128:(m + 1) * 128, :],
                                        dtmp[m % 2][:]), m)

                                @blk.gpsimd
                                def _(gp, m=m):
                                    if m + 2 < NM:
                                        p.wait(gp, m_dmm[m])
                                        load_wd(gp, m + 2)
                            else:
                                @blk.gpsimd
                                def _(gp, m=m):
                                    v.wait(gp, m_dev[m])
                                    m_accw[m] = ga.issue(gp.dma_start(
                                        acc_sp[m * 128:(m + 1) * 128, :],
                                        dtmp[m % 2][:],
                                        accum_op=ALU.add), m)
                                    if m + 2 < NM:
                                        load_wd(gp, m + 2)
                        else:
                            @blk.vector
                            def _(vec, m=m):
                                p.wait(vec, m_dmm[m])
                                ye.wait(vec, m_ep[m])
                                if m >= 2:
                                    a.wait(vec, m_gt[m - 2])
                                m_t2[m] = v.inc(vec.tensor_add(
                                    t2[m % 2][:], pd[m % 2][:],
                                    acc_in[m % 2][:]))

                            @blk.scalar
                            def _(sc, m=m):
                                v.wait(sc, m_t2[m])
                                if m >= 2:
                                    v.wait(sc, m_vout[m - 2])
                                m_gt[m] = a.inc(sc.activation(
                                    gt[m % 2][:], t2[m % 2][:], AF.Silu,
                                    scale=ds_sb[:, m:m + 1]))

                            @blk.vector
                            def _(vec, m=m):
                                a.wait(vec, m_gt[m])
                                if m >= 2:
                                    yw.wait(vec, m_outw[m - 2])
                                i6 = v.inc(vec.tensor_mul(
                                    sg[:], s_in[m % 2][:], gt[m % 2][:]))
                                v.wait(vec, i6)
                                m_vout[m] = v.inc(vec.tensor_add(
                                    out_t[m % 2][:], sg[:], x_in[m % 2][:]))

                            @blk.sync
                            def _(sync, m=m):
                                v.wait(sync, m_vout[m])
                                m_outw[m] = yw.issue(sync.dma_start(
                                    out_fm[m * 128:(m + 1) * 128, :],
                                    out_t[m % 2][:]), m)
                                if m + 2 < NM:
                                    ye.issue(sync.dma_start(
                                        acc_in[m % 2][:],
                                        acc_sp[(m + 2) * 128:(m + 3) * 128, :]),
                                        m + 2)
                                    ye.issue(sync.dma_start(
                                        s_in[m % 2][:],
                                        s_sp[(m + 2) * 128:(m + 3) * 128, :]),
                                        m + 2)
                                    m_ep[m + 2] = ye.issue(sync.dma_start(
                                        x_in[m % 2][:],
                                        x_fm[(m + 2) * 128:(m + 3) * 128, :]),
                                        m + 2)

                            @blk.gpsimd
                            def _(gp, m=m):
                                if m + 2 < NM:
                                    p.wait(gp, m_dmm[m])
                                    load_wd(gp, m + 2)

                    @blk.sync
                    def _(sync):
                        if fgi == 0 or last:
                            yw.wait_all(sync)
                    if fgi == 1:
                        @blk.gpsimd
                        def _(gp):
                            ga.wait_all(gp)

                last_dn_mark[0] = m_dmm[NM - 1]

        with ExitStack() as esFg0:
            h_sb = esFg0.enter_context(
                nc.sbuf_tensor("h_sb0", [128, 29, TOWN], BF16))
            emit_gu(0, xn_t1, None)
            emit_dn(0)
        xn1_es.close()

        with ExitStack() as esA:
            sbA = lambda n, s_, d: esA.enter_context(nc.sbuf_tensor(n, s_, d))
            v_tm = sbA("v_tm", [128, 2, 8, TOWN], BF16)
            attnT = sbA("attnT", [128, 16, TOWN], BF16)
            k_es = ExitStack()
            k_fm = k_es.enter_context(
                nc.sbuf_tensor("k_fm", [128, 2, ND, TOWN], BF16))
            attn_es = ExitStack()
            attn_bf = attn_es.enter_context(
                nc.sbuf_tensor("attn_bf", [128, 8, S], BF16))
            m_kv = [None] * 4
            NQ = TOWN // 128
            m_nrm = [0] * NQ

            with ExitStack() as esS:
                sbs = lambda n, s_, d: esS.enter_context(nc.sbuf_tensor(n, s_, d))
                pss = lambda n, s_, d: esS.enter_context(nc.psum_tensor(n, s_, d))
                negmax = [sbs(f"negmax{i}", [128, 1], F32) for i in range(2)]
                sum_r = [sbs(f"sum_r{i}", [128, 1], F32) for i in range(2)]
                rsum = [sbs(f"rsum{i}", [128, 1], F32) for i in range(2)]
                sc_ps = [pss(f"sc_ps{i}", [128, S], F32) for i in range(2)]

                m_scmm = [0] * NQ
                m_max = [0] * NQ; m_exp = [0] * NQ

                with nc.Block(no_gpsimd_drain=True) as blk:
                    @blk.scalar
                    def _(sc):
                        sc.preload_act_table(AF.Exp)
                        co.wait(sc, 1)
                        pid = sc.partition_id()
                        base = (pid - (pid % 2)) * BLOB
                        pat = [[ND * TOWN, 128], [1, ND * TOWN]]
                        m_kv[1] = zA.issue(sc.dma_start(
                            k_fm[:, 1],
                            bass.AP(kv_all.tensor, base + BLOB, pat)), 0)
                        m_kv[3] = zA.issue(sc.dma_start(
                            v_tm[:, 1],
                            bass.AP(kv_all.tensor, base + BLOB + KN, pat)), 1)

                    @blk.sync
                    def _(sync):
                        co.wait(sync, 1)
                        pid = sync.partition_id()
                        base = (pid - (pid % 2)) * BLOB
                        pat = [[ND * TOWN, 128], [1, ND * TOWN]]
                        m_kv[0] = yA.issue(sync.dma_start(
                            k_fm[:, 0],
                            bass.AP(kv_all.tensor, base, pat)), 0)
                        m_kv[2] = yA.issue(sync.dma_start(
                            v_tm[:, 0],
                            bass.AP(kv_all.tensor, base + KN, pat)), 1)

                    for qc in range(NQ):
                        @blk.tensor
                        def _(te, qc=qc):
                            if qc >= 2:
                                a.wait(te, m_exp[qc - 2])
                            for ks in range(4):
                                if qc == 0:
                                    (yA if ks < 2 else zA).wait(
                                        te, m_kv[ks // 2])
                                for jd in range(ND):
                                    mm = te.matmul(
                                        sc_ps[qc % 2][:, ks * 512:(ks + 1) * 512],
                                        q_sb[:, jd, qc * 128:(qc + 1) * 128],
                                        k_fm[:, ks // 2, jd,
                                             (ks % 2) * 512:(ks % 2) * 512 + 512],
                                        start=(jd == 0), stop=(jd == ND - 1))
                            m_scmm[qc] = p.inc(mm)

                        @blk.vector
                        def _(vec, qc=qc):
                            p.wait(vec, m_scmm[qc])
                            if qc >= 2:
                                a.wait(vec, m_exp[qc - 2])
                            m_max[qc] = v.inc(vec.tensor_reduce(
                                negmax[qc % 2][:], sc_ps[qc % 2][:],
                                AX.X, ALU.max, negate=True))

                        @blk.scalar
                        def _(sc, qc=qc):
                            v.wait(sc, m_max[qc])
                            m_exp[qc] = a.inc(sc.activation(
                                attn_bf[:, qc, :], sc_ps[qc % 2][:], AF.Exp,
                                bias=negmax[qc % 2][:, 0:1],
                                accum_out=sum_r[qc % 2][:, 0:1]))

                        @blk.vector
                        def _(vec, qc=qc):
                            a.wait(vec, m_exp[qc])
                            i5 = v.inc(vec.reciprocal(rsum[qc % 2][:, 0:1],
                                                      sum_r[qc % 2][:, 0:1]))
                            v.wait(vec, i5)
                            m_nrm[qc] = v.inc(vec.tensor_scalar(
                                attn_bf[:, qc, :], attn_bf[:, qc, :],
                                rsum[qc % 2][:, 0:1], None, ALU.mult))

            with ExitStack() as esT:
                psT = lambda n, s_, d: esT.enter_context(nc.psum_tensor(n, s_, d))
                tr2_ps = [psT(f"tr2ps{i}", [128, 128], BF16) for i in range(4)]
                m_tr2 = []; m_tcp = []

                with nc.Block(no_gpsimd_drain=True) as blk:
                    for qc in range(NQ):
                        for kb in range(16):
                            ti = qc * 16 + kb

                            @blk.tensor
                            def _(te, qc=qc, kb=kb, ti=ti):
                                if kb == 0:
                                    v.wait(te, m_nrm[qc])
                                if ti >= 4:
                                    a.wait(te, m_tcp[ti - 4])
                                m_tr2.append(p.inc(te.transpose(
                                    tr2_ps[ti % 4][:],
                                    attn_bf[:, qc, kb * 128:(kb + 1) * 128],
                                    id_sb[:])))

                            @blk.scalar
                            def _(sc, qc=qc, kb=kb, ti=ti):
                                p.wait(sc, m_tr2[ti])
                                m_tcp.append(a.inc(sc.activation(
                                    attnT[:, kb, qc * 128:(qc + 1) * 128],
                                    tr2_ps[ti % 4][:], AF.Copy)))
            attn_es.close()
            k_es.close()
            q_es.close()

            xn2_es = ExitStack()
            xn_own2 = xn2_es.enter_context(
                nc.sbuf_tensor("xn_own2", [128, NA, TOWN], BF16, side="right"))
            m_xn4 = [None] * 4
            pv_es = ExitStack()
            pv_fm = pv_es.enter_context(
                nc.sbuf_tensor("pv_fm", [128, ND, TOWN], BF16, side="right"))
            wo_es = ExitStack()
            wo_buf = [wo_es.enter_context(
                nc.sbuf_tensor(f"wob{i}", [128, ND, 128], BF16, side="right"))
                for i in range(4)]
            m_wo = [None] * NM

            with ExitStack() as esP:
                psp = lambda n, s_, d: esP.enter_context(nc.psum_tensor(n, s_, d))
                pv_ps = [psp(f"pvps{i}", [128, TOWN], F32) for i in range(2)]
                m_pvmm = [0] * ND; m_pvev = [0] * ND

                with nc.Block(no_gpsimd_drain=True) as blk:
                    @blk.gpsimd
                    def _(gp):
                        for mq in range(4):
                            m_wo[mq] = gwo.issue(
                                gp.dma_start(wo_buf[mq][:], wo_st[mq]), mq)

                    @blk.sync
                    def _(sync):
                        yn.wait_all(sync)
                        m_xn4[0] = yA.issue(sync.dma_start(
                            xn_own2[:, 0:8, :], xn_q0[:, 0:8, :]), 0)
                        m_xn4[2] = yA.issue(sync.dma_start(
                            xn_own2[:, 16:24, :], xn_q0[:, 16:24, :]), 1)

                    @blk.scalar
                    def _(sc):
                        m_xn4[1] = zA.issue(sc.dma_start(
                            xn_own2[:, 8:16, :], xn_q0[:, 8:16, :]), 0)
                        m_xn4[3] = zA.issue(sc.dma_start(
                            xn_own2[:, 24:32, :], xn_q0[:, 24:32, :]), 1)

                    for dc in range(ND):
                        @blk.tensor
                        def _(te, dc=dc):
                            if dc == 0:
                                yA.wait(te, m_kv[2])
                                zA.wait(te, m_kv[3])
                            if dc >= 2:
                                v.wait(te, m_pvev[dc - 2])
                            for ks in (0, 512):
                                for kc in range(16):
                                    mm = te.matmul(
                                        pv_ps[dc % 2][:, ks:ks + 512],
                                        v_tm[:, kc // 8, kc % 8,
                                             dc * 128:(dc + 1) * 128],
                                        attnT[:, kc, ks:ks + 512],
                                        start=(kc == 0), stop=(kc == 15))
                            m_pvmm[dc] = p.inc(mm)

                        @blk.vector
                        def _(vec, dc=dc):
                            p.wait(vec, m_pvmm[dc])
                            m_pvev[dc] = v.inc(vec.tensor_copy(
                                pv_fm[:, dc, :], pv_ps[dc % 2][:]))

        with ExitStack() as esO:
            sbo = lambda n, s_, d: esO.enter_context(nc.sbuf_tensor(n, s_, d))
            pso = lambda n, s_, d: esO.enter_context(nc.psum_tensor(n, s_, d))
            s_t = [sbo(f"st{i}", [128, TOWN], BF16) for i in range(2)]
            sp_ps = [pso(f"spps{i}", [128, TOWN], F32) for i in range(2)]
            m_spmm = [0] * NM; m_spev = [0] * NM; m_sw = [None] * NM

            with nc.Block(no_gpsimd_drain=True) as blk:
                for m in range(NM):
                    if m >= 4:
                        @blk.gpsimd
                        def _(gp, m=m):
                            p.wait(gp, m_spmm[m - 4])
                            gwo.wait(gp, m_wo[m - 4])
                            m_wo[m] = gwo.issue(gp.dma_start(
                                wo_buf[m % 4][:], wo_st[m]), m)

                    @blk.tensor
                    def _(te, m=m):
                        gwo.wait(te, m_wo[m])
                        if m >= 2:
                            v.wait(te, m_spev[m - 2])
                        for ks in (0, 512):
                            for dc in range(ND):
                                mm = te.matmul(
                                    sp_ps[m % 2][:, ks:ks + 512],
                                    wo_buf[m % 4][:, dc, :],
                                    pv_fm[:, dc, ks:ks + 512],
                                    start=(dc == 0), stop=(dc == ND - 1))
                        m_spmm[m] = p.inc(mm)

                    @blk.vector
                    def _(vec, m=m):
                        p.wait(vec, m_spmm[m])
                        if m >= 2:
                            yw.wait(vec, m_sw[m - 2])
                        m_spev[m] = v.inc(vec.tensor_scalar(
                            s_t[m % 2][:], sp_ps[m % 2][:],
                            bo_sb[:, m:m + 1], None, ALU.add))

                    @blk.sync
                    def _(sync, m=m):
                        v.wait(sync, m_spev[m])
                        m_sw[m] = yw.issue(sync.dma_start(
                            s_sp[m * 128:(m + 1) * 128, :], s_t[m % 2][:]), m)

                @blk.sync
                def _(sync):
                    yw.wait_all(sync)

        wo_es.close()
        pv_es.close()

        with ExitStack() as esFg1:
            h_sb = esFg1.enter_context(
                nc.sbuf_tensor("h_sb1", [128, 29, TOWN], BF16))
            emit_gu(1, xn_own2, m_xn4)
            emit_dn(1)

        with ExitStack() as esFg2:
            h_sb = esFg2.enter_context(
                nc.sbuf_tensor("h_sb2", [128, 29, TOWN], BF16))
            emit_gu(2, xn_own2, m_xn4)
            xn2_es.close()
            emit_dn(2)

        esH.close()

    return nc





def _stage_weights(Wqkv, bqkv, Wo, bo, gate_w, gate_s, up_w, up_s, down_w, down_s):
    st = {}
    st["wqkv_st"] = np.ascontiguousarray(
        np.asarray(Wqkv, np.float32).reshape(NA, 128, NJ, 128).transpose(2, 1, 0, 3))
    st["bqkv_st"] = np.ascontiguousarray(
        np.asarray(bqkv, np.float32).reshape(NJ, 128).T)
    st["wo_st"] = np.ascontiguousarray(
        np.asarray(Wo, np.float32).reshape(ND, 128, NM, 128).transpose(2, 1, 0, 3))
    st["bo_st"] = np.ascontiguousarray(
        np.asarray(bo, np.float32).reshape(NM, 128).T)
    st["wg_st"] = np.ascontiguousarray(
        np.asarray(gate_w, np.int32).reshape(NF, 128, NA, 128).transpose(0, 3, 2, 1))
    st["wu_st"] = np.ascontiguousarray(
        np.asarray(up_w, np.int32).reshape(NF, 128, NA, 128).transpose(0, 3, 2, 1))
    st["wd_st"] = np.ascontiguousarray(
        np.asarray(down_w, np.int32).reshape(NM, 128, NF, 128).transpose(0, 3, 2, 1))
    st["gs_st"] = np.ascontiguousarray(
        np.asarray(gate_s, np.float32).reshape(NF, 128).T)
    st["us_st"] = np.ascontiguousarray(
        np.asarray(up_s, np.float32).reshape(NF, 128).T)
    st["ds_st"] = np.ascontiguousarray(
        np.asarray(down_s, np.float32).reshape(NM, 128).T)
    st["ident"] = np.eye(128, dtype=np.float32).astype(bf16)
    return st


def kernel(L0_attn_x, L0_attn_Wqkv, L0_attn_bqkv, L0_attn_Wo, L0_attn_bo,
           L0_ffn_gate_w, L0_ffn_gate_s, L0_ffn_up_w, L0_ffn_up_s,
           L0_ffn_down_w, L0_ffn_down_s):
    x = np.asarray(L0_attn_x, np.float32)
    B, Sq, Cm = x.shape
    assert (B, Sq, Cm) == (4, S, C)

    shared = _stage_weights(L0_attn_Wqkv, L0_attn_bqkv, L0_attn_Wo, L0_attn_bo,
                            L0_ffn_gate_w, L0_ffn_gate_s, L0_ffn_up_w,
                            L0_ffn_up_s, L0_ffn_down_w, L0_ffn_down_s)

    in_maps = []
    for core in range(8):
        b, hh = core // 2, core % 2
        m = dict(shared)
        slab = x[b, hh * TOWN:(hh + 1) * TOWN, :]
        m["x_fm"] = np.ascontiguousarray(slab.T)
        m["x_st"] = np.ascontiguousarray(
            slab.reshape(NLT, LT, NA, 128).transpose(0, 3, 2, 1))
        in_maps.append(m)

    nc = build()
    res = run_bass_kernel_spmd(nc, in_maps, core_ids=list(range(8)))

    out = np.empty((B, S, C), np.float32)
    for core in range(8):
        b, hh = core // 2, core % 2
        out[b, hh * TOWN:(hh + 1) * TOWN, :] = res.results[core]["out_fm"].T
    return out
